# version 05
# baseline (speedup 1.0000x reference)
import os
import numpy as np
import concourse.bass as bass
import concourse.mybir as mybir
from concourse.bass_utils import run_bass_kernel_spmd

F32 = mybir.dt.float32
BF16 = mybir.dt.bfloat16
I32 = mybir.dt.int32
AF = mybir.ActivationFunctionType
ALU = mybir.AluOpType
AXX = mybir.AxisListType.X

NCORES = 8
T = 2048
D = 1024
H = 16
HD = 64
HFF = 512
E = 8
MG = 2048
ML = MG // NCORES
NSEQ = 2
NTOK = NSEQ * T
DC = D // 128
HC = HFF // 128

ENGINES = ("sync", "gpsimd", "tensor", "vector", "scalar")

ARW = 60416


class Prog:

    def __init__(self, nc, sems):
        self.nc = nc
        self.sems = sems
        self.counts = {}
        self.mode = "count"

    def phase(self, body):
        snap = dict(self.counts)
        self.mode = "count"
        body(self)
        final = dict(self.counts)
        blk = self.nc.Block()
        with blk:
            for eng in ENGINES:
                def runner(engobj, _eng=eng):
                    self.counts = dict(snap)
                    self.mode = _eng
                    body(self)
                getattr(blk, eng)(runner)
        self.counts = final

    def do(self, eng, fn, inc=None, n=1):
        if self.mode == eng:
            inst = fn(getattr(self.nc, eng))
            if inc is not None and inst is not None:
                inst.then_inc(self.sems[inc], n)
        if inc is not None:
            self.counts[inc] = self.counts.get(inc, 0) + n

    def at(self, ctr):
        return self.counts.get(ctr, 0)

    def wait(self, eng, ctr, val=None):
        v = self.counts.get(ctr, 0) if val is None else val
        if self.mode == eng and v > 0:
            getattr(self.nc, eng).wait_ge(self.sems[ctr], v)


def build_program():
    nc = bass.Bass("TRN2", target_bir_lowering=False, detect_race_conditions=False)

    def din(name, shape, dt=F32):
        return nc.dram_tensor(name, list(shape), dt, kind="ExternalInput")

    x_own = din("x_own", [NTOK, D])
    x_flat = din("x_flat", [16 * T, D])
    wqkvT = din("wqkvT", [D, 3 * D])
    wprojT = din("wprojT", [D, D])
    sw1T = din("sw1T", [D, HFF])
    sw2T = din("sw2T", [D, HFF])
    sw3T = din("sw3T", [HFF, D])
    gwT = din("gwT", [D, 9])
    gb = din("gb", [9])
    expb = din("expb", [E])
    sb1c = din("sb1c", [128, HC])
    sb2c = din("sb2c", [128, HC])
    slgc = din("slgc", [128, HC])
    slbc = din("slbc", [128, HC])
    selgc = din("selgc", [128, HC])
    selbc = din("selbc", [128, HC])
    sb3 = din("sb3", [D])
    eb1c = din("eb1c", [128, E * HC])
    eb2c = din("eb2c", [128, E * HC])
    eb3 = din("eb3", [E, D])
    ew1T = din("ew1T", [E, D, HFF])
    ew2T = din("ew2T", [E, D, HFF])
    ew3T = din("ew3T", [E, HFF, D])
    idxb_all = din("idxb_all", [128, 16], I32)
    idxt_all = din("idxt_all", [128, 16], I32)
    idxb_own = din("idxb_own", [128, 2], I32)
    idxt_own = din("idxt_own", [128, 2], I32)
    basev = din("basev", [128, 1], I32)
    idn = din("idn", [128, 128])
    tri = din("tri", [128, 128])
    repsel = din("repsel", [16, 8 * 128])
    onesc = din("onesc", [128, 1])
    onesr = din("onesr", [1, 128])

    y = nc.dram_tensor("y", [NTOK, D], F32, kind="ExternalOutput")
    dbg = nc.dram_tensor("dbg", [1024, T], F32, kind="ExternalOutput")
    r_bounce = nc.dram_tensor("r_bounce", [ML, D], F32)
    r_all = nc.dram_tensor("r_all", [MG, D], F32)

    def sbuf(name, shape, dt):
        return nc.alloc_sbuf_tensor(name, list(shape), dt)

    wproj_sb = sbuf("wproj_sb", [128, DC * D], BF16)
    sw1_sb = sbuf("sw1_sb", [128, DC * HFF], BF16)
    sw2_sb = sbuf("sw2_sb", [128, DC * HFF], BF16)
    sw3_sb = sbuf("sw3_sb", [128, HC * D], BF16)
    gw_sb = sbuf("gw_sb", [128, DC * 9], BF16)
    sb1_sb = sbuf("sb1_sb", [128, HC], F32)
    sb2_sb = sbuf("sb2_sb", [128, HC], F32)
    slg_sb = sbuf("slg_sb", [128, HC], F32)
    slb_sb = sbuf("slb_sb", [128, HC], F32)
    selg_sb = sbuf("selg_sb", [128, HC], F32)
    selb_sb = sbuf("selb_sb", [128, HC], F32)
    eb1_sb = sbuf("eb1_sb", [128, E * HC], F32)
    eb2_sb = sbuf("eb2_sb", [128, E * HC], F32)
    gbrep_sb = sbuf("gbrep_sb", [128, 9], F32)
    ebrep_sb = sbuf("ebrep_sb", [128, E], F32)
    idn_sb = sbuf("idn_sb", [128, 128], BF16)
    tri_sb = sbuf("tri_sb", [128, 128], BF16)
    repsel_sb = sbuf("repsel_sb", [16, 8 * 128], BF16)
    onesc_sb = sbuf("onesc_sb", [128, 1], BF16)
    onesr_sb = sbuf("onesr_sb", [1, 128], BF16)
    ib_all = sbuf("ib_all", [128, 16], I32)
    it_all = sbuf("it_all", [128, 16], I32)
    ib_own = sbuf("ib_own", [128, 2], I32)
    it_own = sbuf("it_own", [128, 2], I32)
    base_sb = sbuf("base_sb", [128, 1], I32)
    flat_own = sbuf("flat_own", [128, 2], I32)
    loc_all = sbuf("loc_all", [128, 16], I32)
    iw1 = sbuf("iw1", [128, 16], I32)
    iw2 = sbuf("iw2", [128, 16], I32)
    iw3 = sbuf("iw3", [128, 16], I32)
    s_stack = sbuf("s_stack", [16, T], BF16)
    r_stack = sbuf("r_stack", [16, T], BF16)
    b2_stack = sbuf("b2_stack", [16, T], BF16)
    rowc = sbuf("rowc", [1, T], BF16)
    rowd = sbuf("rowd", [1, T], BF16)
    hop16 = sbuf("hop16", [128, T], BF16)
    srow = hop16
    vtot_sb = sbuf("vtot_sb", [128, 8], F32)
    ntmp = sbuf("ntmp", [128, 512], BF16)
    colv = sbuf("colv", [128, 8], F32)
    gsc = sbuf("gsc", [128, 12], F32)
    gts = sbuf("gts", [128, 32], F32)
    maskb = sbuf("maskb", [128, 2 * E], F32)
    wshm = sbuf("wshm", [128, 2], F32)
    wexm = sbuf("wexm", [128, 2], F32)
    xt0 = sbuf("xt0", [128, D], F32)
    xt1 = sbuf("xt1", [128, D], F32)
    epscol = sbuf("epscol", [128, 1], F32)
    rowmu = sbuf("rowmu", [1, 512], F32)
    rowm2 = sbuf("rowm2", [1, 512], F32)

    AR = sbuf("AR", [128, ARW], BF16)

    def car(off, ncols):
        return AR[:, off:off + ncols]

    def car32(off, ncols_f32):
        return AR[:, off:off + 2 * ncols_f32].bitcast(F32)

    XNF_O, APRE_O, VHAT_O, QP_O, KP_O = 0, 16384, 32768, 49408, 51456
    ET0_O, ET1_O, WSL_O, XST_O = 53504, 54528, 55552, 53504
    xnf = [car(XNF_O + c * T, T) for c in range(DC)]
    apre = [car(APRE_O + c * T, T) for c in range(DC)]
    vhat = [car(VHAT_O + k * (H * 65), H * 65) for k in range(16)]
    qpair = car(QP_O, T)
    kpair = car(KP_O, T)
    et0 = car(ET0_O, 1024)
    et1 = car(ET1_O, 1024)
    wsl = car(WSL_O, 4096)
    xstage = car(XST_O, 4096)
    murep = car(QP_O, T)
    rsrep = car(KP_O, T)
    zbuf = [car(VHAT_O + c * T, T) for c in range(HC)]
    shb = [car(VHAT_O + 8192 + c * T, T) for c in range(HC)]
    rw1 = car32(VHAT_O, T)[0:16, :]
    rw2 = car32(VHAT_O + 4096, T)[0:16, :]
    xgv = [car32(c * T, D) for c in range(2)]
    xgn = car(2 * T, 2 * D)
    xgf = [car(3 * T + c * ML, ML) for c in range(DC)]
    ewa = [car(4 * T + c * HFF, HFF) for c in range(DC)]
    ewb = [car(6 * T + c * HFF, HFF) for c in range(DC)]
    ewc = [car(APRE_O + c * D, D) for c in range(HC)]
    eoacc = car32(APRE_O + 4096, 2 * D)
    shsel = car32(APRE_O + 8192, 2 * D)
    rloc = car32(APRE_O + 12288, 2 * D)
    s1m = [car(VHAT_O + c * ML, ML) for c in range(HC)]
    zmm = [car(VHAT_O + 1024 + c * ML, ML) for c in range(HC)]
    z2m = car(VHAT_O + 2048, ML)
    mtmp = car32(VHAT_O + 2560, 512)
    rtile0 = car32(VHAT_O + 4096, D)
    rtile1 = car32(VHAT_O + 6144, D)
    sb3rep_sb = car32(VHAT_O + 8704, D)
    yacc0 = car32(QP_O, D)
    yacc1 = car32(QP_O + 2048, D)
    yout0 = car32(ET0_O, D)
    yout1 = car32(WSL_O, D)
    trash = car32(57600, D)
    z2b2 = car(59648, 512)

    sem_names = ["wl", "al", "xl", "gp", "pe", "ac", "dv", "dg", "ds", "cc", "st", "sc"]
    import contextlib
    ctx = contextlib.ExitStack()
    ctx.enter_context(nc.allow_low_precision(
        "bf16 operands with fp32 psum accumulation are within this problem's measured tolerance"))
    sems = {n: ctx.enter_context(nc.semaphore(n)) for n in sem_names}
    psA = ctx.enter_context(nc.psum_tensor("psA", [128, 1024], F32))
    psB = ctx.enter_context(nc.psum_tensor("psB", [128, 1024], F32))
    psC = ctx.enter_context(nc.psum_tensor("psC", [128, 1024], F32))
    psD = ctx.enter_context(nc.psum_tensor("psD", [128, 1024], F32))
    psA16 = psA.ap().bitcast(BF16)

    p = Prog(nc, sems)

    def ph_load(p):
        for dc in range(DC):
            p.do("gpsimd", lambda g, dc=dc: g.dma_start(
                out=wproj_sb[:, dc * D:(dc + 1) * D], in_=wprojT[dc * 128:(dc + 1) * 128, :]), inc="wl", n=16)
            p.do("gpsimd", lambda g, dc=dc: g.dma_start(
                out=sw1_sb[:, dc * HFF:(dc + 1) * HFF], in_=sw1T[dc * 128:(dc + 1) * 128, :]), inc="wl", n=16)
            p.do("gpsimd", lambda g, dc=dc: g.dma_start(
                out=sw2_sb[:, dc * HFF:(dc + 1) * HFF], in_=sw2T[dc * 128:(dc + 1) * 128, :]), inc="wl", n=16)
            p.do("gpsimd", lambda g, dc=dc: g.dma_start(
                out=gw_sb[:, dc * 9:(dc + 1) * 9], in_=gwT[dc * 128:(dc + 1) * 128, :]), inc="wl", n=16)
        for hc in range(HC):
            p.do("gpsimd", lambda g, hc=hc: g.dma_start(
                out=sw3_sb[:, hc * D:(hc + 1) * D], in_=sw3T[hc * 128:(hc + 1) * 128, :]), inc="wl", n=16)
        p.do("gpsimd", lambda g: g.dma_start(out=idn_sb[:, :], in_=idn[:, :]), inc="wl", n=16)
        p.do("gpsimd", lambda g: g.dma_start(out=tri_sb[:, :], in_=tri[:, :]), inc="wl", n=16)
        p.do("gpsimd", lambda g: g.dma_start(out=repsel_sb[:, :], in_=repsel[:, :]), inc="wl", n=16)
        p.do("gpsimd", lambda g: g.dma_start(out=onesc_sb[:, :], in_=onesc[:, :]), inc="wl", n=16)
        p.do("gpsimd", lambda g: g.dma_start(out=onesr_sb[:, :], in_=onesr[:, :]), inc="wl", n=16)
        for dst, src in ((sb1_sb, sb1c), (sb2_sb, sb2c), (slg_sb, slgc), (slb_sb, slbc),
                         (selg_sb, selgc), (selb_sb, selbc), (eb1_sb, eb1c), (eb2_sb, eb2c),
                         (ib_all, idxb_all), (it_all, idxt_all), (ib_own, idxb_own),
                         (it_own, idxt_own), (base_sb, basev)):
            p.do("sync", lambda s, dst=dst, src=src: s.dma_start(out=dst[:, :], in_=src[:, :]), inc="al", n=16)
        p.do("sync", lambda s: s.dma_start(
            out=gbrep_sb[:, :], in_=bass.AP(gb, 0, [[0, 128], [1, 9]])), inc="al", n=16)
        p.do("sync", lambda s: s.dma_start(
            out=ebrep_sb[:, :], in_=bass.AP(expb, 0, [[0, 128], [1, E]])), inc="al", n=16)
        p.do("sync", lambda s: s.dma_start(
            out=sb3rep_sb[:, :], in_=bass.AP(sb3, 0, [[0, 128], [1, D]])), inc="al", n=16)
        p.do("vector", lambda v: v.memset(epscol[:, :], 1e-5), inc="dv")
        p.wait("gpsimd", "al")
        p.do("gpsimd", lambda g: g.tensor_scalar(iw1[:, 0:2], ib_own[:, :], T, None, ALU.mult))
        p.do("gpsimd", lambda g: g.tensor_tensor(flat_own[:, :], iw1[:, 0:2], it_own[:, :], ALU.add))
        p.do("gpsimd", lambda g: g.tensor_scalar(iw1[:, :], ib_all[:, :], T, None, ALU.mult))
        p.do("gpsimd", lambda g: g.tensor_tensor(iw2[:, :], iw1[:, :], it_all[:, :], ALU.add))
        for j in range(16):
            p.do("gpsimd", lambda g, j=j: g.tensor_tensor(
                loc_all[:, j:j + 1], iw2[:, j:j + 1], base_sb[:, 0:1], ALU.subtract))
        p.do("gpsimd", lambda g: g.tensor_scalar(iw1[:, :], loc_all[:, :], 0, None, ALU.max))
        p.do("gpsimd", lambda g: g.tensor_tensor(iw2[:, :], iw1[:, :], loc_all[:, :], ALU.subtract))
        p.do("gpsimd", lambda g: g.tensor_scalar(iw3[:, :], iw2[:, :], 4096, None, ALU.mult))
        p.do("gpsimd", lambda g: g.tensor_tensor(loc_all[:, :], iw1[:, :], iw3[:, :], ALU.add), inc="gp")

    _SKIP = set(os.environ.get("ATHENA_MK_SKIP", "").split(","))
    p.phase(ph_load)

    def ln_tile(p, src_f32, dst_bf16, n=D):
        p.do("vector", lambda v: v.reduce_sum(colv[:, 0:1], src_f32, axis=AXX), inc="dv")
        p.wait("vector", "dv")
        p.do("vector", lambda v: v.tensor_scalar(colv[:, 1:2], colv[:, 0:1], -1.0 / n, None, ALU.mult), inc="dv")
        p.wait("vector", "dv")
        p.do("vector", lambda v: v.tensor_scalar(trash[:, 0:n], src_f32, colv[:, 1:2], None, ALU.add), inc="dv")
        p.wait("scalar", "dv")
        p.do("scalar", lambda s: s.activation(trash[:, 0:n], trash[:, 0:n], AF.Square,
                                              accum_out=colv[:, 2:3]), inc="ac")
        p.wait("scalar", "ac")
        p.do("scalar", lambda s: s.activation(colv[:, 3:4], colv[:, 2:3], AF.Sqrt,
                                              bias=epscol[:, :], scale=1.0 / n), inc="ac")
        p.wait("vector", "ac")
        p.do("vector", lambda v: v.reciprocal(colv[:, 4:5], colv[:, 3:4]), inc="dv")
        p.wait("vector", "dv")
        p.do("vector", lambda v: v.tensor_scalar(trash[:, 0:n], src_f32, colv[:, 1:2], None, ALU.add), inc="dv")
        p.wait("vector", "dv")
        p.do("vector", lambda v: v.tensor_scalar(dst_bf16, trash[:, 0:n], colv[:, 4:5], None, ALU.mult), inc="dv")

    def swiglu_ln_fm(p, xf, ntok, w1c, w2c, b1col, b2col, gcol, bcol,
                     outbufs, zbufs, z2buf, mur, rsr):
        nspan = (ntok + 511) // 512
        for hc in range(HC):
            for s in range(nspan):
                s0 = s * 512
                w = min(ntok, s0 + 512) - s0
                slot = ((hc * nspan + s) % 2) * 512
                p.wait("tensor", "ac")
                p.wait("tensor", "dv")
                for dc in range(DC):
                    p.do("tensor", lambda t, hc=hc, s0=s0, w=w, dc=dc, slot=slot: t.matmul(
                        psA[:, slot:slot + w], w1c(dc, hc), xf[dc][:, s0:s0 + w],
                        start=(dc == 0), stop=(dc == DC - 1)))
                for dc in range(DC):
                    p.do("tensor", lambda t, hc=hc, s0=s0, w=w, dc=dc, slot=slot: t.matmul(
                        psB[:, slot:slot + w], w2c(dc, hc), xf[dc][:, s0:s0 + w],
                        start=(dc == 0), stop=(dc == DC - 1)), inc="pe")
                p.wait("scalar", "pe")
                p.do("scalar", lambda s_, hc=hc, s0=s0, w=w, slot=slot: s_.activation(
                    outbufs[hc][:, s0:s0 + w], psA[:, slot:slot + w], AF.Silu, bias=b1col(hc)), inc="ac")
                p.wait("vector", "ac")
                p.do("vector", lambda v, hc=hc, s0=s0, w=w, slot=slot: v.scalar_tensor_tensor(
                    zbufs[hc][:, s0:s0 + w], psB[:, slot:slot + w], b2col(hc), outbufs[hc][:, s0:s0 + w],
                    ALU.add, ALU.mult), inc="dv")
        for s in range(nspan):
            s0 = s * 512
            w = min(ntok, s0 + 512) - s0
            p.wait("tensor", "dv")
            p.wait("tensor", "ac")
            for hc in range(HC):
                p.do("tensor", lambda t, s0=s0, w=w, hc=hc: t.matmul(
                    psC[0:1, 0:w], onesc_sb[:, :], zbufs[hc][:, s0:s0 + w],
                    start=(hc == 0), stop=(hc == HC - 1)), inc="pe")
            for hc in range(HC):
                p.wait("vector", "pe")
                p.do("vector", lambda v, s0=s0, w=w, hc=hc: v.tensor_tensor(
                    z2buf[:, 0:w], zbufs[hc][:, s0:s0 + w], zbufs[hc][:, s0:s0 + w], ALU.mult), inc="dv")
                p.wait("tensor", "dv")
                p.do("tensor", lambda t, w=w, hc=hc: t.matmul(
                    psC[0:1, 512:512 + w], onesc_sb[:, :], z2buf[:, 0:w],
                    start=(hc == 0), stop=(hc == HC - 1)), inc="pe")
            p.wait("vector", "pe")
            p.do("vector", lambda v, w=w: v.tensor_scalar(rowmu[0:1, 0:w], psC[0:1, 0:w], 1.0 / HFF, None, ALU.mult))
            p.do("vector", lambda v, w=w: v.tensor_tensor(rowm2[0:1, 0:w], rowmu[0:1, 0:w], rowmu[0:1, 0:w], ALU.mult))
            p.do("vector", lambda v, w=w: v.scalar_tensor_tensor(
                psD[0:1, 512:512 + w], psC[0:1, 512:512 + w], 1.0 / HFF, rowm2[0:1, 0:w],
                ALU.mult, ALU.subtract), inc="dv")
            p.wait("scalar", "dv")
            p.do("scalar", lambda s_, w=w: s_.activation(psD[0:1, 0:w], psD[0:1, 512:512 + w], AF.Sqrt, bias=epscol[0:1, :]), inc="ac")
            p.wait("vector", "ac")
            p.do("vector", lambda v, s0=s0, w=w: v.reciprocal(rowd[0:1, s0:s0 + w], psD[0:1, 0:w]), inc="dv")
            p.wait("vector", "dv")
            p.do("vector", lambda v, s0=s0, w=w: v.tensor_copy(rowc[0:1, s0:s0 + w], rowmu[0:1, 0:w]), inc="dv")
        p.wait("tensor", "dv")
        for s in range(nspan):
            s0 = s * 512
            w = min(ntok, s0 + 512) - s0
            p.do("tensor", lambda t, s0=s0, w=w: t.matmul(
                psA[:, 0:w], onesr_sb[:, :], rowc[0:1, s0:s0 + w], start=True, stop=True))
            p.do("tensor", lambda t, s0=s0, w=w: t.matmul(
                psB[:, 0:w], onesr_sb[:, :], rowd[0:1, s0:s0 + w], start=True, stop=True), inc="pe")
            p.wait("scalar", "pe")
            p.do("scalar", lambda s_, s0=s0, w=w: s_.activation(mur[:, s0:s0 + w], psA[:, 0:w], AF.Copy), inc="ac")
            p.do("scalar", lambda s_, s0=s0, w=w: s_.activation(rsr[:, s0:s0 + w], psB[:, 0:w], AF.Copy), inc="ac")
            p.wait("tensor", "ac")
        p.wait("vector", "ac")
        for hc in range(HC):
            p.do("vector", lambda v, hc=hc: v.tensor_tensor(
                zbufs[hc][:, 0:ntok], zbufs[hc][:, 0:ntok], mur[:, 0:ntok], ALU.subtract))
            p.do("vector", lambda v, hc=hc: v.tensor_tensor(
                zbufs[hc][:, 0:ntok], zbufs[hc][:, 0:ntok], rsr[:, 0:ntok], ALU.mult))
            p.do("vector", lambda v, hc=hc: v.tensor_scalar(
                outbufs[hc][:, 0:ntok], zbufs[hc][:, 0:ntok], gcol(hc), bcol(hc), ALU.mult, ALU.add), inc="dv")

    def ph_moe(p):
        p.wait("gpsimd", "gp")
        for j in range(2):
            p.do("gpsimd", lambda g, j=j: g.indirect_dma_start(
                xgv[j], None,
                x_flat[:, :], bass.IndirectOffsetOnAxis(flat_own[:, j:j + 1], 0)), inc="dg", n=16)
        p.wait("vector", "dg")
        p.wait("scalar", "dg")
        p.wait("vector", "al")
        for j in range(2):
            ln_tile(p, xgv[j], xgn[:, j * D:(j + 1) * D])
        p.wait("tensor", "dv")
        p.wait("tensor", "wl")
        for j in range(2):
            for dc in range(DC):
                p.do("tensor", lambda t, j=j, dc=dc: t.transpose(
                    psA16[:, 0:128],
                    xgn[:, j * D + dc * 128: j * D + dc * 128 + 128], idn_sb[:, :]), inc="pe")
                p.wait("scalar", "pe")
                p.do("scalar", lambda s, j=j, dc=dc: s.activation(
                    xgf[dc][:, j * 128:(j + 1) * 128], psA16[0:128, 0:128], AF.Copy), inc="ac")
                p.wait("tensor", "ac")
        for j in range(2):
            p.wait("tensor", "dv")
            for dc in range(DC):
                p.do("tensor", lambda t, j=j, dc=dc: t.matmul(
                    psB[:, 0:9], xgf[dc][:, j * 128:(j + 1) * 128], gw_sb[:, dc * 9:(dc + 1) * 9],
                    start=(dc == 0), stop=(dc == DC - 1)), inc="pe")
            p.wait("vector", "pe")
            p.do("vector", lambda v, j=j: v.tensor_tensor(gts[:, j * 16:j * 16 + 9], psB[:, 0:9], gbrep_sb[:, :], ALU.add), inc="dv")
            p.wait("scalar", "dv")
            p.do("scalar", lambda s, j=j: s.activation(gts[:, j * 16:j * 16 + 9], gts[:, j * 16:j * 16 + 9], AF.Sigmoid), inc="ac")
            p.wait("vector", "ac")
            p.do("vector", lambda v, j=j: v.tensor_tensor(gsc[:, 0:E], gts[:, j * 16 + 1:j * 16 + 9], ebrep_sb[:, :], ALU.add), inc="dv")
            p.wait("vector", "dv")
            p.do("vector", lambda v, j=j: v.reduce_max(gsc[:, 8:9], gsc[:, 0:E], axis=AXX), inc="dv")
            p.wait("vector", "dv")
            p.do("vector", lambda v, j=j: v.tensor_scalar(maskb[:, j * E:(j + 1) * E], gsc[:, 0:E], gsc[:, 8:9], None, ALU.is_ge), inc="dv")
            p.wait("vector", "dv")
            p.do("vector", lambda v, j=j: v.tensor_tensor(gsc[:, 0:E], maskb[:, j * E:(j + 1) * E], gts[:, j * 16 + 1:j * 16 + 9], ALU.mult), inc="dv")
            p.wait("vector", "dv")
            p.do("vector", lambda v, j=j: v.reduce_sum(gsc[:, 9:10], gsc[:, 0:E], axis=AXX), inc="dv")
            p.wait("vector", "dv")
            p.do("vector", lambda v, j=j: v.tensor_scalar(gsc[:, 10:11], gsc[:, 9:10], -1.0, None, ALU.mult), inc="dv")
            p.wait("vector", "dv")
            p.do("vector", lambda v, j=j: v.tensor_tensor(gsc[:, 11:12], gts[:, j * 16:j * 16 + 1], gsc[:, 10:11], ALU.add), inc="dv")
            p.wait("vector", "dv")
            p.do("vector", lambda v, j=j: v.tensor_scalar(gsc[:, 10:11], gsc[:, 11:12], -1.0, None, ALU.mult), inc="dv")
            p.wait("scalar", "dv")
            p.do("scalar", lambda s, j=j: s.activation(wshm[:, j:j + 1], gsc[:, 11:12], AF.Sigmoid), inc="ac")
            p.do("scalar", lambda s, j=j: s.activation(wexm[:, j:j + 1], gsc[:, 10:11], AF.Sigmoid), inc="ac")
            if "dumpmoe" in _SKIP:
                p.wait("gpsimd", "ac")
                p.do("gpsimd", lambda g, j=j: g.dma_start(out=dbg[896 + j:897 + j, 0:12], in_=gsc[0:1, :]), inc="dg", n=16)
                p.do("gpsimd", lambda g, j=j: g.dma_start(out=dbg[898:899, 2 * j:2 * j + 2], in_=wshm[0:1, :] if j == 0 else wexm[0:1, :]), inc="dg", n=16)
        if "moegates" in _SKIP:
            return
        p.do("vector", lambda v: v.memset(eoacc[:, 0:D], 0.0))
        p.do("vector", lambda v: v.memset(eoacc[:, D:2 * D], 0.0), inc="dv")
        if "moeeb3" in _SKIP:
            return
        ebrep3 = rtile0
        for e in range(E if "moeexp" not in _SKIP else 1):
            p.wait("gpsimd", "pe", p.at("pe"))
            p.wait("gpsimd", "dv", p.at("dv"))
            p.do("gpsimd", lambda g, e=e: g.dma_start(
                out=ebrep3, in_=bass.AP(eb3, e * D, [[0, 128], [1, D]])), inc="dg", n=16)
            for dc in range(DC):
                p.do("gpsimd", lambda g, e=e, dc=dc: g.dma_start(
                    out=ewa[dc], in_=ew1T[e, dc * 128:(dc + 1) * 128, :]), inc="dg", n=16)
                p.do("gpsimd", lambda g, e=e, dc=dc: g.dma_start(
                    out=ewb[dc], in_=ew2T[e, dc * 128:(dc + 1) * 128, :]), inc="dg", n=16)
            for hc in range(HC):
                p.do("gpsimd", lambda g, e=e, hc=hc: g.dma_start(
                    out=ewc[hc], in_=ew3T[e, hc * 128:(hc + 1) * 128, :]), inc="dg", n=16)
            p.wait("tensor", "dg")
            swiglu_ln_fm(p, xgf, ML,
                         lambda dc, hc: ewa[dc][:, hc * 128:(hc + 1) * 128],
                         lambda dc, hc: ewb[dc][:, hc * 128:(hc + 1) * 128],
                         lambda hc, e=e: eb1_sb[:, e * HC + hc:e * HC + hc + 1],
                         lambda hc, e=e: eb2_sb[:, e * HC + hc:e * HC + hc + 1],
                         lambda hc: selg_sb[:, hc:hc + 1],
                         lambda hc: selb_sb[:, hc:hc + 1],
                         s1m, zmm, z2m, mure2_ap, rsre2_ap)
            p.wait("tensor", "dv")
            for j in range(2):
                for s in range(2):
                    for hc in range(HC):
                        p.do("tensor", lambda t, j=j, s=s, hc=hc: t.matmul(
                            psC[:, 0:512], s1m[hc][:, j * 128:(j + 1) * 128],
                            ewc[hc][:, s * 512:(s + 1) * 512],
                            start=(hc == 0), stop=(hc == HC - 1)), inc="pe")
                    p.wait("vector", "pe")
                    p.wait("vector", "dg")
                    p.do("vector", lambda v, s=s: v.tensor_tensor(
                        mtmp[:, 0:512], psC[:, 0:512], ebrep3[:, s * 512:(s + 1) * 512], ALU.add))
                    p.do("vector", lambda v, j=j, e=e: v.tensor_scalar(
                        mtmp[:, 0:512], mtmp[:, 0:512], maskb[:, j * E + e:j * E + e + 1], None, ALU.mult))
                    p.do("vector", lambda v, j=j, s=s: v.tensor_tensor(
                        eoacc[:, j * D + s * 512: j * D + s * 512 + 512],
                        eoacc[:, j * D + s * 512: j * D + s * 512 + 512], mtmp[:, 0:512], ALU.add), inc="dv")
                    p.wait("tensor", "dv")
        if "moenoshared" in _SKIP:
            return
        swiglu_ln_fm(p, xgf, ML,
                     lambda dc, hc: sw1_sb[:, dc * HFF + hc * 128: dc * HFF + hc * 128 + 128],
                     lambda dc, hc: sw2_sb[:, dc * HFF + hc * 128: dc * HFF + hc * 128 + 128],
                     lambda hc: sb1_sb[:, hc:hc + 1],
                     lambda hc: sb2_sb[:, hc:hc + 1],
                     lambda hc: slg_sb[:, hc:hc + 1],
                     lambda hc: slb_sb[:, hc:hc + 1],
                     s1m, zmm, z2m, mure2_ap, rsre2_ap)
        p.wait("tensor", "dv")
        for j in range(2):
            for s in range(2):
                for hc in range(HC):
                    p.do("tensor", lambda t, j=j, s=s, hc=hc: t.matmul(
                        psC[:, 0:512], s1m[hc][:, j * 128:(j + 1) * 128],
                        sw3_sb[:, hc * D + s * 512: hc * D + s * 512 + 512],
                        start=(hc == 0), stop=(hc == HC - 1)), inc="pe")
                p.wait("vector", "pe")
                p.do("vector", lambda v, j=j, s=s: v.tensor_copy(
                    shsel[:, j * D + s * 512: j * D + s * 512 + 512], psC[:, 0:512]), inc="dv")
                p.wait("tensor", "dv")
        p.wait("vector", "al")
        for j in range(2):
            p.do("vector", lambda v, j=j: v.tensor_tensor(
                shsel[:, j * D:(j + 1) * D], shsel[:, j * D:(j + 1) * D], sb3rep_sb[:, :], ALU.add))
            p.do("vector", lambda v, j=j: v.tensor_scalar(wshm[:, j:j + 1], wshm[:, j:j + 1], -1.0, None, ALU.add), inc="dv")
        p.wait("vector", "dv")
        for j in range(2):
            p.do("vector", lambda v, j=j: v.tensor_scalar(
                rloc[:, j * D:(j + 1) * D], shsel[:, j * D:(j + 1) * D], wshm[:, j:j + 1], None, ALU.mult))
            for s in range(2):
                p.do("vector", lambda v, j=j, s=s: v.tensor_scalar(
                    mtmp[:, 0:512], eoacc[:, j * D + s * 512:j * D + s * 512 + 512],
                    wexm[:, j:j + 1], None, ALU.mult))
                p.do("vector", lambda v, j=j, s=s: v.tensor_tensor(
                    rloc[:, j * D + s * 512:j * D + s * 512 + 512],
                    rloc[:, j * D + s * 512:j * D + s * 512 + 512], mtmp[:, 0:512], ALU.add), inc="dv")
        p.wait("sync", "dv")
        for j in range(2):
            p.do("sync", lambda s, j=j: s.dma_start(
                out=r_bounce[j * 128:(j + 1) * 128, :], in_=rloc[:, j * D:(j + 1) * D]), inc="ds", n=16)
        if "dumpmoe" in _SKIP:
            p.wait("gpsimd", "dv")
            p.wait("gpsimd", "ac")
            for j in range(2):
                p.do("gpsimd", lambda g, j=j: g.dma_start(out=dbg[j * 128:(j + 1) * 128, 0:D], in_=rloc[:, j * D:(j + 1) * D]), inc="dg", n=16)
                p.do("gpsimd", lambda g, j=j: g.dma_start(out=dbg[256 + j * 128:256 + (j + 1) * 128, 0:D], in_=eoacc[:, j * D:(j + 1) * D]), inc="dg", n=16)
                p.do("gpsimd", lambda g, j=j: g.dma_start(out=dbg[512 + j * 128:512 + (j + 1) * 128, 0:D], in_=shsel[:, j * D:(j + 1) * D]), inc="dg", n=16)
            p.do("gpsimd", lambda g: g.dma_start(out=dbg[768:896, 0:2], in_=wshm[:, :]), inc="dg", n=16)
            p.do("gpsimd", lambda g: g.dma_start(out=dbg[768:896, 2:4], in_=wexm[:, :]), inc="dg", n=16)
            p.do("gpsimd", lambda g: g.dma_start(out=dbg[768:896, 4:20], in_=maskb[:, :]), inc="dg", n=16)
            p.do("gpsimd", lambda g: g.dma_start(out=dbg[768:896, 20:52], in_=gts[:, :]), inc="dg", n=16)
            p.do("gpsimd", lambda g: g.dma_start(out=dbg[768:896, 52:64], in_=gsc[:, :]), inc="dg", n=16)
            p.wait("gpsimd", "dg")
        if "moenoag" in _SKIP:
            return
        p.wait("gpsimd", "ds")
        p.do("gpsimd", lambda g: g.collective_compute(
            "AllGather", ALU.bypass, replica_groups=[list(range(NCORES))],
            ins=[r_bounce.ap().opt()], outs=[r_all.ap().opt()]), inc="cc")

    mure2_ap = car(VHAT_O + 8192, ML)
    rsre2_ap = car(VHAT_O + 8192 + ML, ML)

    if "moe" not in _SKIP:
        p.phase(ph_moe)

    def ph_seq(p, sq):
        tok0 = sq * T

        p.wait("vector", "st")
        ln_done = {}
        for grp in range(4):
            p.wait("vector", "pe")
            for i in range(4):
                tb = grp * 4 + i
                xt = xt0 if (tb % 2 == 0) else xt1
                if tb >= 2:
                    p.wait("sync", "dv", ln_done[tb - 2])
                p.do("sync", lambda s, tb=tb, xt=xt: s.dma_start(
                    out=xt[:, :], in_=x_own[tok0 + tb * 128: tok0 + tb * 128 + 128, :]), inc="xl", n=16)
                p.wait("vector", "xl")
                p.wait("scalar", "xl")
                ln_tile(p, xt[:, :], xstage[:, i * D:(i + 1) * D])
                ln_done[tb] = p.at("dv")
            p.wait("tensor", "dv")
            for dc in range(DC):
                for i in range(4):
                    p.do("tensor", lambda t, dc=dc, i=i: t.transpose(
                        psA16[:, i * 128:(i + 1) * 128],
                        xstage[:, i * D + dc * 128: i * D + dc * 128 + 128], idn_sb[:, :]), inc="pe")
                p.wait("scalar", "pe")
                p.do("scalar", lambda s, dc=dc, grp=grp: s.activation(
                    xnf[dc][:, grp * 512:(grp + 1) * 512], psA16[:, 0:512], AF.Copy), inc="ac")
                p.wait("tensor", "ac")

        if "dumpxnf" in _SKIP and sq == 0:
            p.wait("gpsimd", "ac")
            for dc in range(DC):
                p.do("gpsimd", lambda g, dc=dc: g.dma_start(
                    out=dbg[dc * 128:(dc + 1) * 128, :], in_=xnf[dc]), inc="dg", n=16)
            p.wait("gpsimd", "dg")
        p.wait("vector", "ac")
        p.wait("vector", "pe")
        for k in range(16):
            p.do("vector", lambda v, k=k: v.memset(
                bass.AP(AR, VHAT_O + k * (H * 65) + 64, [[ARW, 128], [65, H], [1, 1]]), 1.0), inc="dv")
        for half in range(2):
            p.wait("gpsimd", "pe", p.at("pe"))
            p.wait("gpsimd", "st")
            p.wait("gpsimd", "dv", p.at("dv"))
            for dc in range(DC):
                p.do("gpsimd", lambda g, dc=dc, half=half: g.dma_start(
                    out=wsl[:, dc * 512: dc * 512 + 512],
                    in_=wqkvT[dc * 128:(dc + 1) * 128, 2 * D + half * 512: 2 * D + half * 512 + 512]),
                    inc="dg", n=16)
            p.wait("tensor", "dg")
            p.wait("tensor", "dv")
            for tb in range(16):
                p.wait("tensor", "ac")
                for dc in range(DC):
                    p.do("tensor", lambda t, tb=tb, dc=dc: t.matmul(
                        psC[:, 0:512], xnf[dc][:, tb * 128:(tb + 1) * 128],
                        wsl[:, dc * 512: dc * 512 + 512],
                        start=(dc == 0), stop=(dc == DC - 1)), inc="pe")
                p.wait("scalar", "pe")
                dst = bass.AP(AR, VHAT_O + tb * (H * 65) + half * 8 * 65, [[ARW, 128], [65, 8], [1, 64]])
                p.do("scalar", lambda s, dst=dst: s.activation(
                    dst, psC[:, 0:512].rearrange("p (a b) -> p a b", a=8), AF.Copy), inc="ac")
        p.wait("tensor", "ac")
        for pr in range(8):
            h0, h1 = 2 * pr, 2 * pr + 1
            for k in range(16):
                p.do("tensor", lambda t, pr=pr, h0=h0, k=k: t.matmul(
                    psA[0:64, pr:pr + 1], vhat[k][:, h0 * 65: h0 * 65 + 64],
                    onesc_sb[:, :], start=(k == 0), stop=(k == 15)))
            for k in range(16):
                p.do("tensor", lambda t, pr=pr, h1=h1, k=k: t.matmul(
                    psA[64:128, pr:pr + 1], vhat[k][:, h1 * 65: h1 * 65 + 64],
                    onesc_sb[:, :], start=(k == 0), stop=(k == 15)), inc="pe")
        p.wait("vector", "pe")
        p.do("vector", lambda v: v.tensor_copy(vtot_sb[:, 0:8], psA[:, 0:8]), inc="dv")

        for pr in range(8):
            h0, h1 = 2 * pr, 2 * pr + 1
            p.wait("gpsimd", "pe", p.at("pe"))
            for dc in range(DC):
                p.do("gpsimd", lambda g, dc=dc, pr=pr: g.dma_start(
                    out=wsl[:, dc * 256: dc * 256 + 128],
                    in_=wqkvT[dc * 128:(dc + 1) * 128, pr * 128: pr * 128 + 128]), inc="dg", n=16)
                p.do("gpsimd", lambda g, dc=dc, pr=pr: g.dma_start(
                    out=wsl[:, dc * 256 + 128: dc * 256 + 256],
                    in_=wqkvT[dc * 128:(dc + 1) * 128, D + pr * 128: D + pr * 128 + 128]), inc="dg", n=16)
            p.wait("tensor", "dg")
            p.wait("tensor", "dv")
            for s in range(4):
                slot = (s % 2) * 512
                p.wait("tensor", "ac")
                for dc in range(DC):
                    p.do("tensor", lambda t, s=s, dc=dc, slot=slot: t.matmul(
                        psA[:, slot:slot + 512], wsl[:, dc * 256: dc * 256 + 128],
                        xnf[dc][:, s * 512:(s + 1) * 512],
                        start=(dc == 0), stop=(dc == DC - 1)))
                for dc in range(DC):
                    p.do("tensor", lambda t, s=s, dc=dc, slot=slot: t.matmul(
                        psB[:, slot:slot + 512], wsl[:, dc * 256 + 128: dc * 256 + 256],
                        xnf[dc][:, s * 512:(s + 1) * 512],
                        start=(dc == 0), stop=(dc == DC - 1)), inc="pe")
                p.wait("scalar", "pe")
                p.do("scalar", lambda s_, s=s, slot=slot: s_.activation(
                    qpair[:, s * 512:(s + 1) * 512], psA[:, slot:slot + 512], AF.Copy, scale=0.125), inc="ac")
                p.do("scalar", lambda s_, s=s, slot=slot: s_.activation(
                    kpair[:, s * 512:(s + 1) * 512], psB[:, slot:slot + 512], AF.Copy), inc="ac")
            p.wait("tensor", "ac")

            for hi, (h, R0, R1) in enumerate(((h0, 0, 64), (h1, 64, 128))):
                exp_done = []
                av_done = []
                it = 0
                for kc in range(16):
                    q0 = kc * 128
                    qq = q0
                    while qq < T:
                        qe = min(T, (qq // 1024) * 1024 + 1024)
                        qs, span = qq, qe - qq
                        sT = psA if (it % 2 == 0) else psB
                        etb = et0 if (it % 2 == 0) else et1
                        if it >= 2:
                            p.wait("tensor", "ac", exp_done[it - 2])
                        c0 = 0
                        while c0 < span:
                            c1 = min(span, c0 + 512)
                            p.do("tensor", lambda t, R0=R0, R1=R1, kc=kc, qs=qs, c0=c0, c1=c1, sT=sT: t.matmul(
                                sT[:, c0:c1], kpair[R0:R1, kc * 128: kc * 128 + 128],
                                qpair[R0:R1, qs + c0: qs + c1], start=True, stop=True), inc="pe")
                            c0 = c1
                        qk_done = p.at("pe")
                        p.wait("scalar", "pe", qk_done)
                        if it >= 2:
                            p.wait("scalar", "pe", av_done[it - 2])
                        p.do("scalar", lambda s_, span=span, sT=sT, etb=etb: s_.activation(
                            etb[:, 0:span], sT[:, 0:span], AF.Exp), inc="ac")
                        exp_done.append(p.at("ac"))
                        diag = (qs == q0)
                        if diag:
                            p.wait("vector", "ac", exp_done[-1])
                            p.do("vector", lambda v, etb=etb: v.tensor_tensor(
                                etb[:, 0:128], etb[:, 0:128], tri_sb[:, :], ALU.mult), inc="dv")
                            dv_mask = p.at("dv")
                        if diag:
                            p.wait("tensor", "dv", dv_mask)
                        else:
                            p.wait("tensor", "ac", exp_done[-1])
                        seg = qs
                        while seg < qe:
                            segend = min(qe, (seg // 512) * 512 + 512)
                            avp = psC if seg < 1024 else psD
                            last_kc = min(15, (seg // 512) * 4 + 3)
                            p.do("tensor", lambda t, h=h, kc=kc, seg=seg, segend=segend, qs=qs,
                                           avp=avp, etb=etb, st=(kc == 0), sp=(kc == last_kc): t.matmul(
                                avp[0:65, seg % 1024: seg % 1024 + (segend - seg)],
                                vhat[kc][:, h * 65: h * 65 + 65],
                                etb[:, seg - qs: segend - qs], start=st, stop=sp), inc="pe")
                            seg = segend
                        av_done.append(p.at("pe"))
                        it += 1
                        qq = qe
                p.wait("scalar", "pe")
                p.wait("vector", "pe")
                p.wait("vector", "ds")
                hseq = 2 * pr + hi
                if hi == 0:
                    p.do("scalar", lambda s_, pr=pr: s_.activation(apre[pr][0:64, 0:1024], psC[0:64, 0:1024], AF.Copy), inc="ac")
                    p.do("scalar", lambda s_, pr=pr: s_.activation(apre[pr][0:64, 1024:2048], psD[0:64, 0:1024], AF.Copy), inc="ac")
                else:
                    p.wait("scalar", "ds")
                    p.do("scalar", lambda s_: s_.activation(hop16[0:64, 0:1024], psC[0:64, 0:1024], AF.Copy), inc="ac")
                    p.do("scalar", lambda s_: s_.activation(hop16[0:64, 1024:2048], psD[0:64, 0:1024], AF.Copy), inc="ac")
                    p.wait("sync", "ac")
                    p.do("sync", lambda s_, pr=pr: s_.dma_start(out=apre[pr][64:128, :], in_=hop16[0:64, :]), inc="ds", n=16)
                p.do("vector", lambda v: v.tensor_copy(srow[64:65, 0:1024], psC[64:65, 0:1024]))
                p.do("vector", lambda v: v.tensor_copy(srow[64:65, 1024:2048], psD[64:65, 0:1024]), inc="dv")
                p.wait("sync", "dv")
                p.do("sync", lambda s_, hseq=hseq: s_.dma_start(
                    out=s_stack[hseq:hseq + 1, :], in_=srow[64:65, :]), inc="ds", n=16)
                p.wait("tensor", "ac")
                p.wait("tensor", "dv")

        p.wait("vector", "ds")
        p.do("vector", lambda v: v.tensor_scalar(rw1[:, :], s_stack[:, :], 1e-6, None, ALU.add), inc="dv")
        p.wait("vector", "dv")
        p.do("vector", lambda v: v.reciprocal(rw2[:, :], rw1[:, :]), inc="dv")
        p.wait("vector", "dv")
        p.do("vector", lambda v: v.tensor_tensor(rw1[:, :], s_stack[:, :], rw2[:, :], ALU.mult))
        p.do("vector", lambda v: v.tensor_scalar(rw1[:, :], rw1[:, :], T * 1e-8, None, ALU.add), inc="dv")
        p.wait("vector", "dv")
        p.do("vector", lambda v: v.reciprocal(rw1[:, :], rw1[:, :]), inc="dv")
        p.wait("vector", "dv")
        p.do("vector", lambda v: v.tensor_tensor(r_stack[:, :], rw2[:, :], rw1[:, :], ALU.mult))
        p.do("vector", lambda v: v.tensor_scalar(b2_stack[:, :], rw1[:, :], 1e-8, None, ALU.mult), inc="dv")
        p.wait("tensor", "dv")
        for pr in range(8):
            for s in range(4):
                p.do("tensor", lambda t, pr=pr, s=s: t.matmul(
                    psA[:, 0:512], repsel_sb[:, pr * 128:(pr + 1) * 128],
                    r_stack[:, s * 512:(s + 1) * 512], start=True, stop=True))
                p.do("tensor", lambda t, pr=pr, s=s: t.matmul(
                    psB[:, 0:512], repsel_sb[:, pr * 128:(pr + 1) * 128],
                    b2_stack[:, s * 512:(s + 1) * 512], start=True, stop=True), inc="pe")
                p.wait("vector", "pe")
                p.do("vector", lambda v, pr=pr, s=s: v.tensor_tensor(
                    apre[pr][:, s * 512:(s + 1) * 512], apre[pr][:, s * 512:(s + 1) * 512],
                    psA[:, 0:512], ALU.mult))
                p.do("vector", lambda v, pr=pr, s=s: v.tensor_scalar(
                    ntmp[:, 0:512], psB[:, 0:512], vtot_sb[:, pr:pr + 1], None, ALU.mult))
                p.do("vector", lambda v, pr=pr, s=s: v.tensor_tensor(
                    apre[pr][:, s * 512:(s + 1) * 512], apre[pr][:, s * 512:(s + 1) * 512],
                    ntmp[:, 0:512], ALU.add), inc="dv")
                p.wait("tensor", "dv")

        swiglu_ln_fm(p, xnf, T,
                     lambda dc, hc: sw1_sb[:, dc * HFF + hc * 128: dc * HFF + hc * 128 + 128],
                     lambda dc, hc: sw2_sb[:, dc * HFF + hc * 128: dc * HFF + hc * 128 + 128],
                     lambda hc: sb1_sb[:, hc:hc + 1],
                     lambda hc: sb2_sb[:, hc:hc + 1],
                     lambda hc: slg_sb[:, hc:hc + 1],
                     lambda hc: slb_sb[:, hc:hc + 1],
                     shb, zbuf, z2b2, murep, rsrep)

        p.wait("tensor", "dv")
        p.wait("gpsimd", "dv")
        stt_done = {}
        store_done = {}
        for tb in range(16):
            ya = yacc0 if tb % 2 == 0 else yacc1
            yo = yout0 if tb % 2 == 0 else yout1
            if tb >= 2:
                p.wait("gpsimd", "dv", stt_done[tb - 2])
            p.do("gpsimd", lambda g, ya=ya: g.dma_start(
                out=ya[:, :], in_=bass.AP(sb3, 0, [[0, 128], [1, D]])), inc="dg", n=16)
            p.wait("gpsimd", "dg")
            p.do("gpsimd", lambda g, ya=ya, tb=tb: g.dma_start(
                out=ya[:, :], in_=x_own[tok0 + tb * 128: tok0 + tb * 128 + 128, :],
                accum_op=ALU.add), inc="dg", n=16)
            noattn = "noattn" in _SKIP
            noffn = "noffn" in _SKIP
            for s in range(2):
                ps = psC if s == 0 else psD
                if not noattn:
                    for dc in range(DC):
                        p.do("tensor", lambda t, tb=tb, s=s, dc=dc, ps=ps: t.matmul(
                            ps[:, 0:512], apre[dc][:, tb * 128:(tb + 1) * 128],
                            wproj_sb[:, dc * D + s * 512: dc * D + s * 512 + 512],
                            start=(dc == 0), stop=(noffn and dc == DC - 1)))
                if not noffn:
                    for hc in range(HC):
                        p.do("tensor", lambda t, tb=tb, s=s, hc=hc, ps=ps, noattn=noattn: t.matmul(
                            ps[:, 0:512], shb[hc][:, tb * 128:(tb + 1) * 128],
                            sw3_sb[:, hc * D + s * 512: hc * D + s * 512 + 512],
                            start=(noattn and hc == 0), stop=(hc == HC - 1)), inc="pe")
                else:
                    p.do("tensor", lambda t: t.nop(), inc="pe")
            p.wait("vector", "pe")
            p.wait("vector", "dg")
            if tb >= 2:
                p.wait("vector", "st", store_done[tb - 2])
            for s in range(2):
                ps = psC if s == 0 else psD
                p.do("vector", lambda v, ya=ya, yo=yo, s=s, ps=ps: v.scalar_tensor_tensor(
                    yo[:, s * 512:(s + 1) * 512], ps[:, 0:512], 1.0,
                    ya[:, s * 512:(s + 1) * 512], ALU.mult, ALU.add), inc="dv")
            stt_done[tb] = p.at("dv")
            p.wait("sync", "dv")
            p.do("sync", lambda s_, yo=yo, tb=tb: s_.dma_start(
                out=y[tok0 + tb * 128: tok0 + tb * 128 + 128, :], in_=yo[:, :]), inc="st", n=16)
            store_done[tb] = p.at("st")
            p.wait("tensor", "dv", stt_done[tb])

    for sq in range(NSEQ):
        if f"seq{sq}" not in _SKIP:
            p.phase(lambda pp, sq=sq: ph_seq(pp, sq))

    def ph_scatter(p):
        p.wait("gpsimd", "cc")
        p.wait("gpsimd", "st")
        for j in range(16):
            rt = rtile0 if j % 2 == 0 else rtile1
            if j >= 2:
                p.wait("gpsimd", "sc", 16 * (j - 1))
            p.do("gpsimd", lambda g, j=j, rt=rt: g.dma_start(
                out=rt, in_=r_all[j * 128:(j + 1) * 128, :]), inc="dg", n=16)
            p.wait("gpsimd", "dg")
            p.do("gpsimd", lambda g, j=j, rt=rt: g.indirect_dma_start(
                y[:, :], bass.IndirectOffsetOnAxis(loc_all[:, j:j + 1], 0),
                rt, None, compute_op=ALU.add,
                bounds_check=NTOK - 1, oob_is_err=False), inc="sc", n=16)
        p.wait("gpsimd", "sc")

    if "scatter" not in _SKIP and "moe" not in _SKIP:
        p.phase(ph_scatter)

    ctx.close()
    return nc


def kernel(**inputs):
    i = {k: np.asarray(v) for k, v in inputs.items()}
    x = np.ascontiguousarray(i["L0_attn_x"], dtype=np.float32)
    x_flat = x.reshape(16 * T, D)

    def ct(a):
        return np.ascontiguousarray(a, dtype=np.float32)

    wqkvT = ct(i["L0_attn_w_qkv"].T)
    wprojT = ct(i["L0_attn_w_proj"].T)
    sw1T = ct(i["L0_ffn_sw1"].T)
    sw2T = ct(i["L0_ffn_sw2"].T)
    sw3T = ct(i["L0_ffn_sw3"].T)
    gwT = ct(np.concatenate([i["L0_ffn_sg_w"], i["L0_ffn_eg_w"]], axis=0).T)
    gb = ct(np.concatenate([i["L0_ffn_sg_b"], i["L0_ffn_eg_b"]], axis=0))
    expb = ct(i["L0_ffn_exp_bias"])

    def colchunks(v):
        return ct(np.asarray(v).reshape(HC, 128).T)

    sb1c = colchunks(i["L0_ffn_sb1"])
    sb2c = colchunks(i["L0_ffn_sb2"])
    slgc = colchunks(i["L0_ffn_s_ln_g"])
    slbc = colchunks(i["L0_ffn_s_ln_b"])
    selgc = colchunks(i["L0_ffn_sel_ln_g"])
    selbc = colchunks(i["L0_ffn_sel_ln_b"])
    sb3 = ct(i["L0_ffn_sb3"])
    eb1c = ct(i["L0_ffn_eb1"].reshape(E, HC, 128).transpose(2, 0, 1).reshape(128, E * HC))
    eb2c = ct(i["L0_ffn_eb2"].reshape(E, HC, 128).transpose(2, 0, 1).reshape(128, E * HC))
    eb3 = ct(i["L0_ffn_eb3"])
    ew1T = ct(i["L0_ffn_ew1"].transpose(0, 2, 1))
    ew2T = ct(i["L0_ffn_ew2"].transpose(0, 2, 1))
    ew3T = ct(i["L0_ffn_ew3"].transpose(0, 2, 1))

    ib = np.ascontiguousarray(i["L0_ffn_index_b"], dtype=np.int32)
    it = np.ascontiguousarray(i["L0_ffn_index_t"], dtype=np.int32)
    idxb_all = np.ascontiguousarray(ib.reshape(16, 128).T)
    idxt_all = np.ascontiguousarray(it.reshape(16, 128).T)

    idn = np.eye(128, dtype=np.float32)
    tri = np.ascontiguousarray(
        (np.arange(128)[None, :] >= np.arange(128)[:, None]).astype(np.float32))
    repsel = np.zeros((16, 8 * 128), np.float32)
    for pr in range(8):
        repsel[2 * pr, pr * 128: pr * 128 + 64] = 1.0
        repsel[2 * pr + 1, pr * 128 + 64: pr * 128 + 128] = 1.0
    onesc = np.ones((128, 1), np.float32)
    onesr = np.ones((1, 128), np.float32)

    nc = build_program()
    in_maps = []
    for c in range(NCORES):
        m = dict(
            x_own=np.ascontiguousarray(x_flat[c * NTOK:(c + 1) * NTOK]),
            x_flat=x_flat,
            wqkvT=wqkvT, wprojT=wprojT, sw1T=sw1T, sw2T=sw2T, sw3T=sw3T,
            gwT=gwT, gb=gb, expb=expb,
            sb1c=sb1c, sb2c=sb2c, slgc=slgc, slbc=slbc, selgc=selgc, selbc=selbc,
            sb3=sb3, eb1c=eb1c, eb2c=eb2c, eb3=eb3,
            ew1T=ew1T, ew2T=ew2T, ew3T=ew3T,
            idxb_all=idxb_all, idxt_all=idxt_all,
            idxb_own=np.ascontiguousarray(idxb_all[:, 2 * c:2 * c + 2]),
            idxt_own=np.ascontiguousarray(idxt_all[:, 2 * c:2 * c + 2]),
            basev=np.full((128, 1), c * NTOK, np.int32),
            idn=idn, tri=tri, repsel=repsel, onesc=onesc, onesr=onesr,
        )
        in_maps.append(m)
    res = run_bass_kernel_spmd(nc, in_maps, core_ids=list(range(NCORES)))
    globals()["_LAST"] = res
    out = np.concatenate([res.results[c]["y"].reshape(NSEQ, T, D) for c in range(NCORES)], axis=0)
    return out
